# version 37
# baseline (speedup 1.0000x reference)
import contextlib
import numpy as np
import ml_dtypes

import concourse.bass as bass
import concourse.mybir as mybir
from concourse.bass_utils import run_bass_kernel_spmd

F32 = mybir.dt.float32
BF16 = mybir.dt.bfloat16
AF = mybir.ActivationFunctionType
OP = mybir.AluOpType
AX = mybir.AxisListType

D = 1024
NF = 8
ROWS = 2048
T = 256
U = T // 128
NTILES = ROWS // T
NCORES = 8

MAGIC = float(np.float32(12582912.0))
SCALE = 1.0 / float(np.float32(np.sqrt(128.0)))
NEG_INF = float(np.finfo(np.float32).min)
INV_D = 1.0 / 1024.0
INV_QMAX = float(np.float32(1.0) / np.float32(127.0))
CB = [0, 16]


class Ctr:

    def __init__(self, handle):
        self.h = handle
        self.n = 0

    def inc(self, instr, k=1):
        instr.then_inc(self.h, k)
        self.n += k
        return instr

    def wait(self, eng, n=None):
        eng.wait_ge(self.h, self.n if n is None else n)


def build_nc(n_tiles=NTILES, debug=False):
    nc = bass.Bass(trn_type="TRN2", target_bir_lowering=False,
                   detect_race_conditions=False)
    rows = n_tiles * T

    x_d = nc.dram_tensor("x", [rows, D], F32, kind="ExternalInput").ap()
    y_d = nc.dram_tensor("y", [rows, D], F32, kind="ExternalOutput").ap()
    wT_names = ["wqkT", "wvT", "wqT", "wkT", "wv2T", "woT"]
    wT_d = {n: nc.dram_tensor(n, [D, D], F32, kind="ExternalInput").ap()
            for n in wT_names}
    wffn_d = nc.dram_tensor("wffn", [D, D], F32, kind="ExternalInput").ap()
    b_names = ["bqk", "bv", "bq", "bk", "bv2", "bo", "bffn"]
    b_d = {n: nc.dram_tensor(n, [128, NF], F32, kind="ExternalInput").ap()
           for n in b_names}
    ident_d = nc.dram_tensor("ident", [128, 128], F32, kind="ExternalInput").ap()
    identb_d = nc.dram_tensor("identb", [128, 128], BF16, kind="ExternalInput").ap()
    mask_d = nc.dram_tensor("maskx4", [128, 512], BF16, kind="ExternalInput").ap()
    ones_d = nc.dram_tensor("ones1", [1, 128], F32, kind="ExternalInput").ap()

    es = contextlib.ExitStack()
    sb = lambda name, shape, dt=F32: es.enter_context(nc.sbuf_tensor(name, shape, dt))
    psb = lambda name: es.enter_context(nc.psum_tensor(name, [128, 512], F32))

    wr = {n: sb("r_" + n, [128, NF * D], BF16) for n in wT_names}
    wr["wffnT"] = sb("r_wffnT", [128, NF * D], BF16)
    bias = {n: sb("s_" + n, [128, NF]) for n in b_names}
    wsc = sb("wsc", [128, NF])
    ident = sb("ident_s", [128, 128])
    identb = sb("identb_s", [128, 128], BF16)
    maskx4 = sb("maskx4_s", [128, 512], BF16)
    ones1 = sb("ones1_s", [1, 128])

    x_buf = [sb(f"x{b}", [128, U * D]) for b in range(2)]
    xn_tok = [sb(f"xn{c}", [128, D]) for c in range(U)]
    st = sb("stats", [128, 32])
    xs4 = sb("xs4", [128, 4])
    xsT = sb("xsT", [4, 128])
    xsrow = sb("xsrow", [1, 512])
    invxs = sb("invxs", [128, 512])
    PB = [psb(f"pb{i}") for i in range(6)]
    PB += [es.enter_context(nc.psum_tensor(f"pbb{i}", [128, 1024], BF16))
           for i in (6, 7)]
    guard = {i: None for i in range(8)}

    sT = Ctr(es.enter_context(nc.semaphore("semT")))
    sV = Ctr(es.enter_context(nc.semaphore("semV")))
    sS = Ctr(es.enter_context(nc.semaphore("semS")))
    sX = Ctr(es.enter_context(nc.semaphore("semX")))
    sW = Ctr(es.enter_context(nc.semaphore("semW")))
    sC = Ctr(es.enter_context(nc.semaphore("semC")))
    sR = Ctr(es.enter_context(nc.semaphore("semR")))
    sO = Ctr(es.enter_context(nc.semaphore("semO")))

    SY, TE, V, S, G = nc.sync, nc.tensor, nc.vector, nc.scalar, nc.gpsimd

    from concourse.hw_specs import get_activation_tables
    _tabs = get_activation_tables(nc.m.arch)
    NATX_ID = [i for i, v in enumerate(_tabs.values())
               if AF.Ln in v and AF.Exp in v][0]

    def load_natural_exp():
        S.add_instruction(mybir.InstLoadActFuncSet(
            name=nc.get_next_instruction_name(),
            act_func_set_id=NATX_ID, ins=[], outs=[]))

    def te_guard(b):
        g = guard[b]
        if g is not None:
            for ctr, n in (g if isinstance(g, list) else [g]):
                ctr.wait(TE, n)
            guard[b] = None

    state = {"ln_guard": None,
             "xn_ready": None,
             "xn_done": [None, None],
             "at_tr": None}

    def emit_ln_early(tt):
        xb = x_buf[tt % 2]
        xc = [xb[:, c * D:(c + 1) * D] for c in range(U)]

        def mk_E0(c, fn, col, last):
            def E0():
                if c == 0 and fn is AF.Square:
                    S.wait_ge(sX.h, 16 * (tt + 1))
                    if state["ln_guard"] is not None:
                        S.wait_ge(sT.h, state["ln_guard"])
                ins = S.activation(xn_tok[c][:], xc[c], fn,
                                   accum_out=st[:, CB[c] + col:CB[c] + col + 1])
                if last:
                    sS.inc(ins)
                    state["ln_sums"] = sS.n
            return E0

        def E1():
            V.wait_ge(sS.h, state["ln_sums"])
            for c in range(U):
                V.tensor_scalar(st[:, CB[c] + 2:CB[c] + 3],
                                st[:, CB[c]:CB[c] + 1], INV_D, None, OP.mult)
            for c in range(U):
                V.tensor_scalar(st[:, CB[c] + 3:CB[c] + 4],
                                st[:, CB[c] + 1:CB[c] + 2], INV_D, None, OP.mult)
            for c in range(U):
                V.tensor_scalar(st[:, CB[c] + 4:CB[c] + 5],
                                st[:, CB[c] + 2:CB[c] + 3],
                                st[:, CB[c] + 2:CB[c] + 3], None, OP.mult)
            for c in range(U):
                V.tensor_sub(st[:, CB[c] + 5:CB[c] + 6],
                             st[:, CB[c] + 3:CB[c] + 4], st[:, CB[c] + 4:CB[c] + 5])
            for c in range(U):
                ins = V.tensor_scalar(st[:, CB[c] + 6:CB[c] + 7],
                                      st[:, CB[c] + 5:CB[c] + 6], 1e-5, None,
                                      OP.add)
            sV.inc(ins)
            state["ln_a"] = sV.n

        def E2():
            S.wait_ge(sV.h, state["ln_a"])
            for c in range(U):
                S.activation(st[:, CB[c] + 7:CB[c] + 8],
                             st[:, CB[c] + 6:CB[c] + 7], AF.Ln)
            for c in range(U):
                ins = S.activation(st[:, CB[c] + 8:CB[c] + 9],
                                   st[:, CB[c] + 7:CB[c] + 8], AF.Exp, scale=-0.5)
            sS.inc(ins)
            state["ln_r0"] = sS.n

        return [mk_E0(0, AF.Square, 1, False), mk_E0(1, AF.Square, 1, False),
                mk_E0(0, AF.Identity, 0, False), mk_E0(1, AF.Identity, 0, True),
                E1, E2]

    def emit_ln_late(tt):
        xb = x_buf[tt % 2]
        xc = [xb[:, c * D:(c + 1) * D] for c in range(U)]

        def L0():
            V.wait_ge(sS.h, state["ln_r0"])
            for c in range(U):
                V.tensor_scalar(st[:, CB[c] + 9:CB[c] + 10],
                                st[:, CB[c] + 8:CB[c] + 9],
                                st[:, CB[c] + 8:CB[c] + 9], None, OP.mult)
            for c in range(U):
                V.tensor_scalar(st[:, CB[c] + 10:CB[c] + 11],
                                st[:, CB[c] + 9:CB[c] + 10],
                                st[:, CB[c] + 6:CB[c] + 7], None, OP.mult)
            for c in range(U):
                V.tensor_scalar(st[:, CB[c] + 11:CB[c] + 12],
                                st[:, CB[c] + 10:CB[c] + 11], -0.5, 1.5,
                                OP.mult, OP.add)
            for c in range(U):
                V.tensor_scalar(st[:, CB[c] + 8:CB[c] + 9],
                                st[:, CB[c] + 11:CB[c] + 12],
                                st[:, CB[c] + 8:CB[c] + 9], None, OP.mult)
            for c in range(U):
                ins = V.tensor_scalar(st[:, CB[c] + 12:CB[c] + 13],
                                      st[:, CB[c] + 2:CB[c] + 3],
                                      st[:, CB[c] + 8:CB[c] + 9], -1.0,
                                      OP.mult, OP.mult)
            sV.inc(ins)
            state["rstd_done"] = sV.n

        def mk_xnS(c):
            def xnS():
                if c == 0:
                    S.wait_ge(sV.h, state["rstd_done"])
                ins = S.activation(xn_tok[c][:], xc[c], AF.Identity,
                                   scale=st[:, CB[c] + 8:CB[c] + 9],
                                   bias=st[:, CB[c] + 12:CB[c] + 13])
                sS.inc(ins)
                state["xn_done"][c] = sS.n
                if c == 1:
                    load_natural_exp()
            return xnS

        def L3():
            V.wait_ge(sS.h, state["xn_done"][0])
            V.reduce_max(st[:, CB[0] + 13:CB[0] + 14], xn_tok[0][:], axis=AX.X,
                         apply_absolute_value=True)

        def L4():
            V.wait_ge(sS.h, state["xn_done"][1])
            V.reduce_max(st[:, CB[1] + 13:CB[1] + 14], xn_tok[1][:], axis=AX.X,
                         apply_absolute_value=True)

        def L5():
            for c in range(U):
                V.tensor_scalar(st[:, CB[c] + 14:CB[c] + 15],
                                st[:, CB[c] + 13:CB[c] + 14], 1e-8, None, OP.max)
            for c in range(U):
                V.reciprocal(st[:, CB[c] + 10:CB[c] + 11],
                             st[:, CB[c] + 14:CB[c] + 15])
            for c in range(U):
                V.tensor_scalar(xs4[:, c:c + 1], st[:, CB[c] + 10:CB[c] + 11],
                                127.0, None, OP.mult)
            for c in range(U):
                ins = V.tensor_scalar(xs4[:, 2 + c:3 + c],
                                      st[:, CB[c] + 14:CB[c] + 15], INV_QMAX,
                                      None, OP.mult)
            sV.inc(ins)
            state["xn_ready"] = sV.n

        return [L0, mk_xnS(0), mk_xnS(1), L3, L4, L5]

    def emit_ln(t):
        ea0, ea1, eb0, eb1, e1, e2 = emit_ln_early(t)
        l0, x0, x1, l3, l4, l5 = emit_ln_late(t)
        ea0(); ea1(); eb0(); eb1(); e1(); e2(); l0()
        S.activation(st[:, 7:8], st[:, 7:8], AF.Copy)
        x0(); x1(); l3(); l4(); l5()

    sX.inc(SY.dma_start(
        x_buf[0][:].rearrange("p (c d) -> p c d", d=D),
        x_d[0:T, :].rearrange("(c p) d -> p c d", p=128)), 16)

    for ap_s, ap_d in ((ident, ident_d), (identb, identb_d), (maskx4, mask_d),
                       (ones1, ones_d)):
        sC.inc(SY.dma_start(ap_s[:], ap_d[:]), 16)
    consts_ready = sC.n

    for n in wT_names:
        for kc in range(NF):
            sW.inc(G.dma_start(wr[n][:, kc * D:(kc + 1) * D],
                               wT_d[n][kc * 128:(kc + 1) * 128, :]), 16)
    w_ready = {n: 16 * NF * (i + 1) for i, n in enumerate(wT_names)}

    xn_fm32 = sb("xn_fm32", [128, NF * T])
    xn_fmb = sb("xn_fmb", [128, NF * T], BF16)
    xq_fmb = sb("xq_fmb", [128, NF * T], BF16)
    qk_b = sb("qk_b", [128, NF * T], BF16)
    vin_b = sb("vin_b", [128, NF * T], BF16)
    q_b = sb("q_b", [128, NF * T], BF16)
    k_b = sb("k_b", [128, NF * T], BF16)
    vtok_b = sb("vtok_b", [128, U * D], BF16)
    g_s = sb("g_s", [128, NF * T])
    o_b = sb("o_b", [128, NF * T], BF16)
    tmp_fm = [sb(f"tmp_fm{i}", [128, T]) for i in range(2)]
    sums = sb("sums", [128, 8])
    recs = sb("recs", [128, 8])
    A_b = sb("A_b", [128, 1024], BF16)
    AT_s = sb("AT_s", [128, NF * 128], BF16)
    bfrow = sb("bfrow", [1, 1024])
    bfold = sb("bfold", [128, NF])
    sg_s = [sb(f"sg{i}", [128, T], BF16) for i in range(3)]

    stg = [g_s[:, 0:D], g_s[:, D:2 * D]]
    wqn = [o_b[:, 0:D], o_b[:, D:2 * D]]
    sC.inc(SY.dma_start(stg[0], wffn_d[0:128, :]), 16)
    sC.inc(SY.dma_start(stg[1], wffn_d[128:256, :]), 16)
    stg_ready = sC.n
    if n_tiles > 1:
        sX.inc(SY.dma_start(
            x_buf[1][:].rearrange("p (c d) -> p c d", d=D),
            x_d[T:2 * T, :].rearrange("(c p) d -> p c d", p=128)), 16)
    for n in b_names:
        sC.inc(SY.dma_start(bias[n][:], b_d[n][:]), 16)

    load_natural_exp()
    emit_ln(0)

    pstate = {"q": None, "tr": None, "stg": (sC, stg_ready)}

    def prep_pair(op):
        if op > 0:
            Q, ctr = (SY, sC) if op == 2 else (G, sW)
            if pstate["q"] is not None:
                Q.wait_ge(sV.h, pstate["q"])
            for j in (0, 1):
                ctr.inc(Q.dma_start(stg[j],
                                    wffn_d[(op + j) * 128:(op + j + 1) * 128, :]),
                        16)
            pstate["stg"] = (ctr, ctr.n)
        sctr, scnt = pstate["stg"]
        V.wait_ge(sctr.h, scnt)
        for j in (0, 1):
            V.reduce_max(st[:, CB[j] + 13:CB[j] + 14], stg[j], axis=AX.X,
                         apply_absolute_value=True)
        for j in (0, 1):
            V.tensor_scalar(st[:, CB[j] + 14:CB[j] + 15],
                            st[:, CB[j] + 13:CB[j] + 14], 1e-8, None, OP.max)
        for j in (0, 1):
            V.reciprocal(st[:, CB[j] + 10:CB[j] + 11],
                         st[:, CB[j] + 14:CB[j] + 15])
        for j in (0, 1):
            V.tensor_scalar(st[:, CB[j] + 9:CB[j] + 10],
                            st[:, CB[j] + 10:CB[j] + 11], 127.0, None, OP.mult)
        for j in (0, 1):
            V.tensor_scalar(wsc[:, op + j:op + j + 1],
                            st[:, CB[j] + 14:CB[j] + 15], INV_QMAX, None, OP.mult)
        if pstate["tr"] is not None:
            V.wait_ge(sT.h, pstate["tr"])
        for j in (0, 1):
            V.tensor_scalar(stg[j], stg[j], st[:, CB[j] + 9:CB[j] + 10],
                            MAGIC, OP.mult, OP.add)
        for j in (0, 1):
            ins = V.tensor_scalar(wqn[j], stg[j], -MAGIC, None, OP.add)
        sV.inc(ins)
        pstate["q"] = sV.n

        sV.wait(TE)
        if op == 0:
            TE.wait_ge(sC.h, 16 * 4)
        for j in (0, 1):
            oc = op + j
            bk = 6 + j
            te_guard(bk)
            for kc in range(NF):
                ins = TE.transpose(PB[bk][:, kc * 128:(kc + 1) * 128],
                                   wqn[j][kc * 128:(kc + 1) * 128]
                                   if False else
                                   o_b[:, j * D + kc * 128: j * D + (kc + 1) * 128],
                                   identb[:])
            sT.inc(ins)
            sT.wait(S)
            dst = wr["wffnT"][:].rearrange("p (k d) -> p k d", d=D)[
                :, :, oc * 128:(oc + 1) * 128]
            sS.inc(S.activation(dst, PB[bk][:].rearrange("p (k d) -> p k d",
                                                         d=128), AF.Copy))
            guard[bk] = (sS, sS.n)
        pstate["tr"] = sT.n
        state["ffn_ready_T"] = sS.n
        state["ffn_ready_V"] = sV.n

    prep_pair(0)
    prep_parts = [lambda op=op: prep_pair(op) for op in (2, 4, 6)]

    for t in range(n_tiles):
        xb = x_buf[t % 2]
        xn_ready = state["xn_ready"]

        TE.wait_ge(sV.h, xn_ready)
        te_guard(4)
        sT.inc(TE.transpose(PB[4][0:4, 0:128], xs4[:], ident[:]))
        state["ln_guard"] = sT.n
        sT.wait(S)
        sS.inc(S.activation(xsT[:], PB[4][0:4, 0:128], AF.Copy))
        guard[4] = (sS, sS.n)
        sS.wait(SY)
        sR.inc(SY.dma_start(xsrow[:], xsT[:]), 16)

        xn_ev = []
        XNB = [2, 3, 0, 1]
        for fp in range(4):
            b = XNB[fp]
            te_guard(b)
            for j in range(2):
                f = fp * 2 + j
                for c in range(U):
                    ins = TE.transpose(
                        PB[b][:, j * 256 + c * 128: j * 256 + c * 128 + 128],
                        xn_tok[c][:, f * 128:(f + 1) * 128], ident[:])
            sT.inc(ins)
            sT.wait(S)
            f0 = fp * 2
            ins1 = S.activation(xn_fmb[:, f0 * T:(f0 + 2) * T], PB[b][:],
                                AF.Copy)
            sS.inc(ins1)
            xn_ev.append(sS.n)
            ins2 = S.activation(xn_fm32[:, f0 * T:(f0 + 2) * T], PB[b][:],
                                AF.Copy)
            sS.inc(ins2)
            guard[b] = (sS, sS.n)
        xn_fm_ready = xn_ev[-1]
        xn32_ready = sS.n

        def proj(dst, wname, bname, rhs_buf, kc_waits=None, rot=0, rhs_ev=None):
            if kc_waits is None:
                TE.wait_ge(sS.h, rhs_ev)
            for m in range(NF):
                b = 2 + ((m + rot) % 4)
                te_guard(b)
                for kc in range(NF):
                    if kc_waits is not None and m == 0:
                        TE.wait_ge(sS.h, kc_waits[kc // 2])
                    ins = TE.matmul(
                        PB[b][:, 0:T],
                        wr[wname][:, kc * D + m * 128: kc * D + m * 128 + 128],
                        rhs_buf[:, kc * T:(kc + 1) * T],
                        start=(kc == 0), stop=(kc == NF - 1))
                sT.inc(ins)
                sT.wait(S)
                sS.inc(S.activation(dst[:, m * T:(m + 1) * T], PB[b][:, 0:T],
                                    AF.Identity, bias=bias[bname][:, m:m + 1]))
                guard[b] = (sS, sS.n)

        if t == 0:
            sC.wait(S)
        TE.wait_ge(sW.h, w_ready["wqkT"])
        proj(qk_b, "wqkT", "bqk", xn_fmb, kc_waits=xn_ev)
        qk_ev = sS.n

        te_guard(2)
        sR.wait(TE)
        for c in range(U):
            TE.matmul(PB[2][:, c * 128:(c + 1) * 128], ones1[:],
                      xsrow[0:1, c * 128:(c + 1) * 128], start=True, stop=True)
            ins = TE.matmul(PB[2][:, 256 + c * 128:256 + (c + 1) * 128], ones1[:],
                            xsrow[0:1, 256 + c * 128:256 + (c + 1) * 128],
                            start=True, stop=True)
        sT.inc(ins)
        sT.wait(V)
        sV.inc(V.tensor_copy(invxs[:], PB[2][:]))
        guard[2] = (sV, sV.n)
        invxs_ready = sV.n

        V.wait_ge(sV.h, invxs_ready)
        V.wait_ge(sS.h, xn32_ready)
        V.tensor_scalar(st[:, 11:12], st[:, 11:12], 1.0, None, OP.mult)
        for m in range(NF):
            V.tensor_mul(tmp_fm[m % 2][:], xn_fm32[:, m * T:(m + 1) * T],
                         invxs[:, 0:T])
            if m > 0:
                V.tensor_scalar(xq_fmb[:, (m - 1) * T:m * T],
                                tmp_fm[(m - 1) % 2][:], MAGIC, -MAGIC,
                                OP.add, OP.add)
        sV.inc(V.tensor_scalar(xq_fmb[:, (NF - 1) * T:NF * T],
                               tmp_fm[(NF - 1) % 2][:], MAGIC, -MAGIC,
                               OP.add, OP.add))
        xq_ready = sV.n

        ln_early = emit_ln_early(t + 1) if t + 1 < n_tiles else []
        if ln_early:
            ln_early[0]()
        TE.wait_ge(sW.h, w_ready["wvT"])
        proj(vin_b, "wvT", "bv", xn_fmb, rot=1, rhs_ev=xn_fm_ready)
        vin_ev = sS.n
        if t == 0:
            prep_parts.pop(0)()
        if ln_early:
            ln_early[1]()
        TE.wait_ge(sW.h, w_ready["wqT"])
        proj(q_b, "wqT", "bq", qk_b, rhs_ev=qk_ev)
        if t == 0:
            prep_parts.pop(0)()
        if ln_early:
            ln_early[2]()
        TE.wait_ge(sW.h, w_ready["wkT"])
        proj(k_b, "wkT", "bk", qk_b, rhs_ev=qk_ev)
        k_ev = sS.n
        if t == 0:
            prep_parts.pop(0)()
        if ln_early:
            ln_early[3]()
            ln_early[4]()

        TE.wait_ge(sW.h, w_ready["wv2T"])
        TE.wait_ge(sS.h, vin_ev)
        for c in range(U):
            for half in range(2):
                b = 2 + ((2 * c + half) % 4)
                te_guard(b)
                for kc in range(NF):
                    ins = TE.matmul(
                        PB[b][:, 0:512],
                        vin_b[:, kc * T + c * 128: kc * T + c * 128 + 128],
                        wr["wv2T"][:, kc * D + half * 512: kc * D + half * 512 + 512],
                        start=(kc == 0), stop=(kc == NF - 1))
                sT.inc(ins)
                sT.wait(S)
                sS.inc(S.activation(
                    vtok_b[:, c * D + half * 512: c * D + half * 512 + 512],
                    PB[b][:, 0:512], AF.Copy))
                guard[b] = (sS, sS.n)

        TE.wait_ge(sS.h, k_ev)
        expb = {0: xn_fm32[:, 0:1024], 1: xn_fm32[:, 1024:2048]}

        def scores(c):
            for hh in range(2):
                te_guard(4 + hh)
                TE.matmul(PB[4 + hh][:, 0:512], identb[:], maskx4[:],
                          start=True, stop=False)
                for j in range(4):
                    h = hh * 4 + j
                    ins = TE.matmul(PB[4 + hh][:, j * 128:(j + 1) * 128],
                                    q_b[:, h * T + c * 128: h * T + c * 128 + 128],
                                    k_b[:, h * T + c * 128: h * T + c * 128 + 128],
                                    start=False, stop=(j == 3))
                sT.inc(ins)
            return sT.n

        def expc(c, sc_cnt):
            S.wait_ge(sT.h, sc_cnt)
            exs = []
            for hh in range(2):
                ins = S.activation(expb[c][:, hh * 512:(hh + 1) * 512],
                                   PB[4 + hh][:], AF.Exp, scale=SCALE)
                sS.inc(ins)
                exs.append(sS.n)
                guard[4 + hh] = (sS, sS.n)
            return exs

        def soft2(c, exp_cnts):
            V.wait_ge(sS.h, exp_cnts[0])
            V.reduce_sum(sums[:, 0:4],
                         expb[c][:, 0:512].rearrange("p (a b) -> p a b", a=4),
                         axis=AX.X)
            V.wait_ge(sS.h, exp_cnts[1])
            V.reduce_sum(sums[:, 4:8],
                         expb[c][:, 512:1024].rearrange("p (a b) -> p a b",
                                                        a=4), axis=AX.X)
            V.reciprocal(recs[:, 0:4], sums[:, 0:4])
            V.reciprocal(recs[:, 4:8], sums[:, 4:8])
            if state["at_tr"] is not None:
                V.wait_ge(sT.h, state["at_tr"])
            nms = []
            for hh in range(2):
                for j in range(4):
                    ins = V.tensor_scalar(A_b[:, hh * 512 + j * 128:
                                              hh * 512 + (j + 1) * 128],
                                          expb[c][:, hh * 512 + j * 128:
                                               hh * 512 + (j + 1) * 128],
                                          recs[:, 4 * hh + j:4 * hh + j + 1],
                                          None, OP.mult)
                sV.inc(ins)
                nms.append(sV.n)
            return nms

        def at_tr_f(c, norm_cnts):
            trs = []
            for hh in range(2):
                TE.wait_ge(sV.h, norm_cnts[hh])
                te_guard(6 + hh)
                for j in range(4):
                    ins = TE.transpose(PB[6 + hh][:, j * 128:(j + 1) * 128],
                                       A_b[:, hh * 512 + j * 128:
                                           hh * 512 + (j + 1) * 128], identb[:])
                sT.inc(ins)
                trs.append(sT.n)
            state["at_tr"] = sT.n
            return trs

        def at_ev(c, tr_cnts):
            evs = []
            for hh in range(2):
                S.wait_ge(sT.h, tr_cnts[hh])
                ins = S.activation(AT_s[:, hh * 512:(hh + 1) * 512],
                                   PB[6 + hh][:, 0:512], AF.Copy)
                sS.inc(ins)
                evs.append(sS.n)
                guard[6 + hh] = (sS, sS.n)
            return evs

        def o_mm(c, ev_cnts):
            for hh in range(2):
                TE.wait_ge(sS.h, ev_cnts[hh])
                te_guard(2 + hh)
                for j in range(4):
                    h = hh * 4 + j
                    ins = TE.matmul(PB[2 + hh][:, j * 128:(j + 1) * 128],
                                    vtok_b[:, c * D + h * 128: c * D + h * 128 + 128],
                                    AT_s[:, h * 128:(h + 1) * 128],
                                    start=True, stop=True)
                sT.inc(ins)
            return sT.n

        def o_ev(c, mm_cnt):
            S.wait_ge(sT.h, mm_cnt)
            for hh in range(2):
                src_ = PB[2 + hh][:].rearrange("p (j x) -> p j x", x=128)
                dst_ = o_b[:].rearrange("p (h x) -> p h x", x=T)[
                    :, 4 * hh:4 * hh + 4, c * 128:(c + 1) * 128]
                ins = S.activation(dst_, src_, AF.Copy)
            sS.inc(ins)
            guard[2] = (sS, sS.n)
            guard[3] = (sS, sS.n)
            return sS.n

        ffn_mm_cnt = {}
        ffn_sil = {}
        ffn_slot_free = {}
        ffn_last_mm = {0: None, 1: None}

        def ffn_slot(m):
            return m % 2, (T if (m % 4) >= 2 else 0)

        def ffn_mm(m):
            if t == 0 and m == 0:
                TE.wait_ge(sS.h, state["ffn_ready_T"])
            if m == 0:
                TE.wait_ge(sV.h, xq_ready)
            b, co = ffn_slot(m)
            if m < 2:
                te_guard(b)
            elif m >= 4:
                TE.wait_ge(sV.h, ffn_slot_free[m % 4])
            for kc in range(NF):
                ins = TE.matmul(
                    PB[b][:, co:co + T],
                    wr["wffnT"][:, kc * D + m * 128: kc * D + m * 128 + 128],
                    xq_fmb[:, kc * T:(kc + 1) * T],
                    start=(kc == 0), stop=(kc == NF - 1))
            sT.inc(ins)
            ffn_mm_cnt[m] = sT.n
            ffn_last_mm[b] = sT.n

        def ffn_ev(m):
            if t == 0 and m == 0:
                V.wait_ge(sV.h, state["ffn_ready_V"])
            b, co = ffn_slot(m)
            V.wait_ge(sT.h, ffn_last_mm[b])
            sV.inc(V.scalar_tensor_tensor(g_s[:, m * T:(m + 1) * T],
                                          PB[b][:, co:co + T],
                                          wsc[:, m:m + 1], invxs[:, 256:256 + T],
                                          OP.mult, OP.mult))
            ffn_slot_free[m % 4] = sV.n
            guard[b] = (sV, sV.n)

        def ffn_si(m):
            sV.wait(S)
            sS.inc(S.activation(g_s[:, m * T:(m + 1) * T],
                                g_s[:, m * T:(m + 1) * T],
                                AF.Silu, bias=bias["bffn"][:, m:m + 1]))
            ffn_sil[m] = sS.n

        if ln_early:
            ln_early[5]()
        sc0 = scores(0)
        ffn_mm(0)
        ffn_mm(1)
        ex0 = expc(0, sc0)
        nm0 = soft2(0, ex0)
        sc1 = scores(1)
        ffn_mm(2)
        ffn_mm(3)
        tr0 = at_tr_f(0, nm0)
        ex1 = expc(1, sc1)
        ffn_ev(0)
        ffn_ev(1)
        ev0 = at_ev(0, tr0)
        S.preload_act_table(AF.Silu)
        ffn_mm(4)
        ffn_mm(5)
        om0 = o_mm(0, ev0)
        nm1 = soft2(1, ex1)
        oe0 = o_ev(0, om0)
        ffn_si(0)
        ffn_si(1)
        tr1 = at_tr_f(1, nm1)
        ffn_ev(2)
        ffn_ev(3)
        ev1 = at_ev(1, tr1)
        ffn_mm(6)
        om1 = o_mm(1, ev1)
        ffn_mm(7)
        ffn_ev(4)
        ffn_ev(5)
        oe1 = o_ev(1, om1)
        o_ready = sS.n
        ffn_si(2)
        ffn_si(3)
        ffn_ev(6)
        ffn_ev(7)
        ffn_si(4)
        ffn_si(5)
        ffn_si(6)
        ffn_si(7)

        if t == 0:
            TE.wait_ge(sW.h, w_ready["woT"])
            if state["at_tr"] is not None:
                V.wait_ge(sT.h, state["at_tr"])
            sC.wait(V)
            sV.inc(V.tensor_copy(A_b[:, 0:8], bias["bv2"][:]))
            sV.wait(TE)
            te_guard(2)
            for half in range(2):
                if half == 1:
                    TE.wait_ge(sS.h, sS.n)
                for kc in range(NF):
                    ins = TE.matmul(
                        PB[2][0:1, 0:512], A_b[:, kc:kc + 1],
                        wr["woT"][:, kc * D + half * 512: kc * D + half * 512 + 512],
                        start=(kc == 0), stop=(kc == NF - 1))
                sT.inc(ins)
                sT.wait(S)
                sS.inc(S.activation(bfrow[0:1, half * 512:(half + 1) * 512],
                                    PB[2][0:1, 0:512], AF.Copy))
            guard[2] = (sS, sS.n)
            sS.wait(TE)
            te_guard(3)
            for kc in range(NF):
                ins = TE.transpose(PB[3][:, kc:kc + 1],
                                   bfrow[0:1, kc * 128:(kc + 1) * 128],
                                   ones1[0:1, 0:1])
            sT.inc(ins)
            sT.wait(V)
            sV.inc(V.tensor_add(bfold[:], PB[3][:, 0:8], bias["bo"][:]))
            guard[3] = (sV, sV.n)
            V.tensor_scalar(st[:, 11:12], st[:, 11:12], 1.0, None, OP.mult)
        ln_late = emit_ln_late(t + 1) if t + 1 < n_tiles else []
        TE.wait_ge(sW.h, w_ready["woT"])
        TE.wait_ge(sS.h, o_ready)
        sg_tr = {}
        sg_stt = {}

        def sg_transpose(m):
            TE.wait_ge(sV.h, sg_stt[m])
            if m == 0:
                te_guard(6)
                te_guard(7)
            for c in range(U):
                ins = TE.transpose(PB[6 + c][:, m * 128:(m + 1) * 128],
                                   sg_s[m % 3][:, c * 128:(c + 1) * 128],
                                   identb[:])
            sT.inc(ins)
            sg_tr[m] = sT.n

        for m in range(NF):
            b = 2 + (m % 2)
            te_guard(b)
            for kc in range(NF):
                ins = TE.matmul(
                    PB[b][:, 0:T],
                    wr["woT"][:, kc * D + m * 128: kc * D + m * 128 + 128],
                    o_b[:, kc * T:(kc + 1) * T],
                    start=(kc == 0), stop=(kc == NF - 1))
            sT.inc(ins)
            if m >= 2:
                sg_transpose(m - 2)
            sT.wait(V)
            V.wait_ge(sS.h, ffn_sil[m])
            if m >= 3:
                V.wait_ge(sT.h, sg_tr[m - 3])
            sV.inc(V.scalar_tensor_tensor(sg_s[m % 3][:], PB[b][:, 0:T],
                                          bfold[:, m:m + 1],
                                          g_s[:, m * T:(m + 1) * T],
                                          OP.add, OP.mult))
            guard[b] = (sV, sV.n)
            sg_stt[m] = sV.n
            if ln_late:
                ln_late.pop(0)()
        sg_transpose(NF - 2)
        sg_transpose(NF - 1)
        while ln_late:
            ln_late.pop(0)()
        sT.wait(V)
        for c in range(U):
            ins = V.tensor_add(xb[:, c * D:(c + 1) * D],
                               xb[:, c * D:(c + 1) * D], PB[6 + c][:])
        sV.inc(ins)
        guard[6] = (sV, sV.n)
        guard[7] = (sV, sV.n)
        fin_done = sV.n

        SY.wait_ge(sV.h, fin_done)
        sO.inc(SY.dma_start(
            y_d[t * T:(t + 1) * T, :].rearrange("(c p) d -> p c d", p=128),
            xb[:].rearrange("p (c d) -> p c d", d=D)), 16)
        if t + 2 < n_tiles:
            SY.wait_ge(sO.h, 16 * (t + 1))
            sX.inc(SY.dma_start(
                x_buf[t % 2][:].rearrange("p (c d) -> p c d", d=D),
                x_d[(t + 2) * T:(t + 3) * T, :].rearrange(
                    "(c p) d -> p c d", p=128)), 16)

    if debug:
        dbg = {
            "d_xn0": xn_tok[0], "d_xn1": xn_tok[1], "d_xnfmb": xn_fmb,
            "d_xnfm32": xn_fm32, "d_xqfmb": xq_fmb, "d_qk": qk_b,
            "d_vin": vin_b, "d_q": q_b, "d_k": k_b, "d_vtok": vtok_b,
            "d_o": o_b, "d_g": g_s, "d_invxs": invxs, "d_Ab": A_b,
            "d_ATs": AT_s, "d_xs4": xs4, "d_sums": sums,
            "d_wffnT": wr["wffnT"], "d_wsc": wsc, "d_st": st,
        }
        SY.wait_ge(sT.h, sT.n)
        SY.wait_ge(sV.h, sV.n)
        SY.wait_ge(sS.h, sS.n)
        sD = Ctr(es.enter_context(nc.semaphore("semD")))
        for nm, tt in dbg.items():
            shape = list(tt.ap().shape)
            dd = nc.dram_tensor(nm, shape, tt.dtype, kind="ExternalOutput").ap()
            sD.inc(SY.dma_start(dd[:], tt[:]), 16)
        SY.wait_ge(sD.h, sD.n)
    SY.wait_ge(sO.h, 16 * n_tiles)
    es.close()
    return nc


def _host_inputs(inputs):
    f32 = np.float32
    x = np.ascontiguousarray(np.asarray(inputs["L0_attn_x"], f32).reshape(-1, D))
    qk_w = np.asarray(inputs["L0_attn_proj_qk_w"], f32)
    v_w = np.asarray(inputs["L0_attn_proj_v_w"], f32)
    in_w = np.asarray(inputs["L0_attn_in_proj_w"], f32)
    out_w = np.asarray(inputs["L0_attn_out_proj_w"], f32)
    ffn_w = np.ascontiguousarray(np.asarray(inputs["L0_ffn_weight"], f32))
    tr = lambda w: np.ascontiguousarray(w.T)
    bre = lambda b: np.ascontiguousarray(np.asarray(b, f32).reshape(NF, 128).T)
    in_b = np.asarray(inputs["L0_attn_in_proj_b"], f32)

    mask = np.full((128, 128), -1.0e30, f32)
    for q in range(128):
        s0 = (q // 4) * 4
        for k in range(s0, s0 + 4):
            if k != q:
                mask[q, k] = 0.0
    common = {
        "wqkT": tr(qk_w), "wvT": tr(v_w),
        "wqT": tr(in_w[0:D]), "wkT": tr(in_w[D:2 * D]),
        "wv2T": tr(in_w[2 * D:3 * D]),
        "woT": tr(out_w), "wffn": ffn_w,
        "bqk": bre(inputs["L0_attn_proj_qk_b"]),
        "bv": bre(inputs["L0_attn_proj_v_b"]),
        "bq": bre(in_b[0:D]), "bk": bre(in_b[D:2 * D]), "bv2": bre(in_b[2 * D:3 * D]),
        "bo": bre(inputs["L0_attn_out_proj_b"]), "bffn": bre(inputs["L0_ffn_bias"]),
        "ident": np.eye(128, dtype=f32),
        "identb": np.eye(128, dtype=f32).astype(ml_dtypes.bfloat16),
        "maskx4": np.ascontiguousarray(np.tile(mask, (1, 4))).astype(
            ml_dtypes.bfloat16),
        "ones1": np.ones((1, 128), f32),
    }
    return x, common


def kernel(**inputs):
    x, common = _host_inputs(inputs)
    nc = build_nc(NTILES)
    in_maps = []
    for i in range(NCORES):
        m = dict(common)
        m["x"] = np.ascontiguousarray(x[i * ROWS:(i + 1) * ROWS])
        in_maps.append(m)
    res = run_bass_kernel_spmd(nc, in_maps, core_ids=list(range(NCORES)))
    y = np.concatenate([np.asarray(res.results[i]["y"]) for i in range(NCORES)],
                       axis=0)
    return y.reshape(2, 8192, D)


# version 42
# speedup vs baseline: 1.0509x; 1.0509x over previous
import contextlib
import numpy as np
import ml_dtypes

import concourse.bass as bass
import concourse.mybir as mybir
from concourse.bass_utils import run_bass_kernel_spmd

F32 = mybir.dt.float32
BF16 = mybir.dt.bfloat16
AF = mybir.ActivationFunctionType
OP = mybir.AluOpType
AX = mybir.AxisListType

D = 1024
NF = 8
ROWS = 2048
T = 256
U = T // 128
NTILES = ROWS // T
NCORES = 8

MAGIC = float(np.float32(12582912.0))
SCALE = 1.0 / float(np.float32(np.sqrt(128.0)))
NEG_INF = float(np.finfo(np.float32).min)
INV_D = 1.0 / 1024.0
INV_QMAX = float(np.float32(1.0) / np.float32(127.0))
CB = [0, 16]


class Ctr:

    def __init__(self, handle):
        self.h = handle
        self.n = 0

    def inc(self, instr, k=1):
        instr.then_inc(self.h, k)
        self.n += k
        return instr

    def wait(self, eng, n=None):
        eng.wait_ge(self.h, self.n if n is None else n)


def build_nc(n_tiles=NTILES, debug=False):
    nc = bass.Bass(trn_type="TRN2", target_bir_lowering=False,
                   detect_race_conditions=False)
    rows = n_tiles * T

    x_d = nc.dram_tensor("x", [rows, D], F32, kind="ExternalInput").ap()
    y_d = nc.dram_tensor("y", [rows, D], F32, kind="ExternalOutput").ap()
    wT_names = ["wqkT", "wvT", "wqT", "wkT", "wv2T", "woT"]
    wT_d = {n: nc.dram_tensor(n, [D, D], F32, kind="ExternalInput").ap()
            for n in wT_names}
    wffn_d = nc.dram_tensor("wffn", [D, D], F32, kind="ExternalInput").ap()
    b_names = ["bqk", "bv", "bq", "bk", "bv2", "bo", "bffn"]
    b_d = {n: nc.dram_tensor(n, [128, NF], F32, kind="ExternalInput").ap()
           for n in b_names}
    ident_d = nc.dram_tensor("ident", [128, 128], F32, kind="ExternalInput").ap()
    identb_d = nc.dram_tensor("identb", [128, 128], BF16, kind="ExternalInput").ap()
    mask_d = nc.dram_tensor("maskx4", [128, 512], BF16, kind="ExternalInput").ap()
    ones_d = nc.dram_tensor("ones1", [1, 128], F32, kind="ExternalInput").ap()

    es = contextlib.ExitStack()
    sb = lambda name, shape, dt=F32: es.enter_context(nc.sbuf_tensor(name, shape, dt))
    psb = lambda name: es.enter_context(nc.psum_tensor(name, [128, 512], F32))

    wr = {n: sb("r_" + n, [128, NF * D], BF16) for n in wT_names}
    wr["wffnT"] = sb("r_wffnT", [128, NF * D], BF16)
    bias = {n: sb("s_" + n, [128, NF]) for n in b_names}
    wsc = sb("wsc", [128, NF])
    ident = sb("ident_s", [128, 128])
    identb = sb("identb_s", [128, 128], BF16)
    maskx4 = sb("maskx4_s", [128, 512], BF16)
    ones1 = sb("ones1_s", [1, 128])

    x_buf = [sb(f"x{b}", [128, U * D]) for b in range(2)]
    xn_tok = [sb(f"xn{c}", [128, D]) for c in range(U)]
    st = sb("stats", [128, 32])
    xs4 = sb("xs4", [128, 4])
    xsT = sb("xsT", [4, 128])
    xsrow = sb("xsrow", [1, 512])
    invxs = sb("invxs", [128, 512])
    PB = [psb(f"pb{i}") for i in range(6)]
    PB += [es.enter_context(nc.psum_tensor(f"pbb{i}", [128, 1024], BF16))
           for i in (6, 7)]
    guard = {i: None for i in range(8)}

    sT = Ctr(es.enter_context(nc.semaphore("semT")))
    sV = Ctr(es.enter_context(nc.semaphore("semV")))
    sS = Ctr(es.enter_context(nc.semaphore("semS")))
    sX = Ctr(es.enter_context(nc.semaphore("semX")))
    sW = Ctr(es.enter_context(nc.semaphore("semW")))
    sC = Ctr(es.enter_context(nc.semaphore("semC")))
    sR = Ctr(es.enter_context(nc.semaphore("semR")))
    sO = Ctr(es.enter_context(nc.semaphore("semO")))

    SY, TE, V, S, G = nc.sync, nc.tensor, nc.vector, nc.scalar, nc.gpsimd

    from concourse.hw_specs import get_activation_tables
    _tabs = get_activation_tables(nc.m.arch)
    NATX_ID = [i for i, v in enumerate(_tabs.values())
               if AF.Ln in v and AF.Exp in v][0]

    def load_natural_exp():
        S.add_instruction(mybir.InstLoadActFuncSet(
            name=nc.get_next_instruction_name(),
            act_func_set_id=NATX_ID, ins=[], outs=[]))

    def te_guard(b):
        g = guard[b]
        if g is not None:
            for ctr, n in (g if isinstance(g, list) else [g]):
                ctr.wait(TE, n)
            guard[b] = None

    state = {"ln_guard": None,
             "xn_ready": None,
             "xn_done": [None, None],
             "at_tr": None}

    def emit_ln_early(tt):
        xb = x_buf[tt % 2]
        xc = [xb[:, c * D:(c + 1) * D] for c in range(U)]

        def mk_E0(c, fn, col, last):
            def E0():
                if c == 0 and fn is AF.Square:
                    S.wait_ge(sX.h, 16 * (tt + 1))
                    if state["ln_guard"] is not None:
                        S.wait_ge(sT.h, state["ln_guard"])
                ins = S.activation(xn_tok[c][:], xc[c], fn,
                                   accum_out=st[:, CB[c] + col:CB[c] + col + 1])
                if last:
                    sS.inc(ins)
                    state["ln_sums"] = sS.n
            return E0

        def E1():
            V.wait_ge(sS.h, state["ln_sums"])
            for c in range(U):
                V.tensor_scalar(st[:, CB[c] + 2:CB[c] + 3],
                                st[:, CB[c]:CB[c] + 1], INV_D, None, OP.mult)
            for c in range(U):
                V.tensor_scalar(st[:, CB[c] + 3:CB[c] + 4],
                                st[:, CB[c] + 1:CB[c] + 2], INV_D, None, OP.mult)
            for c in range(U):
                V.tensor_scalar(st[:, CB[c] + 4:CB[c] + 5],
                                st[:, CB[c] + 2:CB[c] + 3],
                                st[:, CB[c] + 2:CB[c] + 3], None, OP.mult)
            for c in range(U):
                V.tensor_sub(st[:, CB[c] + 5:CB[c] + 6],
                             st[:, CB[c] + 3:CB[c] + 4], st[:, CB[c] + 4:CB[c] + 5])
            for c in range(U):
                ins = V.tensor_scalar(st[:, CB[c] + 6:CB[c] + 7],
                                      st[:, CB[c] + 5:CB[c] + 6], 1e-5, None,
                                      OP.add)
            sV.inc(ins)
            state["ln_a"] = sV.n

        def E2():
            S.wait_ge(sV.h, state["ln_a"])
            for c in range(U):
                S.activation(st[:, CB[c] + 7:CB[c] + 8],
                             st[:, CB[c] + 6:CB[c] + 7], AF.Ln)
            for c in range(U):
                ins = S.activation(st[:, CB[c] + 8:CB[c] + 9],
                                   st[:, CB[c] + 7:CB[c] + 8], AF.Exp, scale=-0.5)
            sS.inc(ins)
            state["ln_r0"] = sS.n

        return [mk_E0(0, AF.Square, 1, False), mk_E0(1, AF.Square, 1, False),
                mk_E0(0, AF.Identity, 0, False), mk_E0(1, AF.Identity, 0, True),
                E1, E2]

    def emit_ln_late(tt):
        xb = x_buf[tt % 2]
        xc = [xb[:, c * D:(c + 1) * D] for c in range(U)]

        def L0():
            V.wait_ge(sS.h, state["ln_r0"])
            for c in range(U):
                V.tensor_scalar(st[:, CB[c] + 9:CB[c] + 10],
                                st[:, CB[c] + 8:CB[c] + 9],
                                st[:, CB[c] + 8:CB[c] + 9], None, OP.mult)
            for c in range(U):
                V.tensor_scalar(st[:, CB[c] + 10:CB[c] + 11],
                                st[:, CB[c] + 9:CB[c] + 10],
                                st[:, CB[c] + 6:CB[c] + 7], None, OP.mult)
            for c in range(U):
                V.tensor_scalar(st[:, CB[c] + 11:CB[c] + 12],
                                st[:, CB[c] + 10:CB[c] + 11], -0.5, 1.5,
                                OP.mult, OP.add)
            for c in range(U):
                V.tensor_scalar(st[:, CB[c] + 8:CB[c] + 9],
                                st[:, CB[c] + 11:CB[c] + 12],
                                st[:, CB[c] + 8:CB[c] + 9], None, OP.mult)
            for c in range(U):
                ins = V.tensor_scalar(st[:, CB[c] + 12:CB[c] + 13],
                                      st[:, CB[c] + 2:CB[c] + 3],
                                      st[:, CB[c] + 8:CB[c] + 9], -1.0,
                                      OP.mult, OP.mult)
            sV.inc(ins)
            state["rstd_done"] = sV.n

        def mk_xnS(c):
            def xnS():
                if c == 0:
                    S.wait_ge(sV.h, state["rstd_done"])
                ins = S.activation(xn_tok[c][:], xc[c], AF.Identity,
                                   scale=st[:, CB[c] + 8:CB[c] + 9],
                                   bias=st[:, CB[c] + 12:CB[c] + 13])
                sS.inc(ins)
                state["xn_done"][c] = sS.n
            return xnS

        def L3():
            V.wait_ge(sS.h, state["xn_done"][0])
            V.reduce_max(st[:, CB[0] + 13:CB[0] + 14], xn_tok[0][:], axis=AX.X,
                         apply_absolute_value=True)

        def L4():
            V.wait_ge(sS.h, state["xn_done"][1])
            V.reduce_max(st[:, CB[1] + 13:CB[1] + 14], xn_tok[1][:], axis=AX.X,
                         apply_absolute_value=True)

        def L5():
            for c in range(U):
                V.tensor_scalar(st[:, CB[c] + 14:CB[c] + 15],
                                st[:, CB[c] + 13:CB[c] + 14], 1e-8, None, OP.max)
            for c in range(U):
                V.reciprocal(st[:, CB[c] + 10:CB[c] + 11],
                             st[:, CB[c] + 14:CB[c] + 15])
            for c in range(U):
                V.tensor_scalar(xs4[:, c:c + 1], st[:, CB[c] + 10:CB[c] + 11],
                                127.0, None, OP.mult)
            for c in range(U):
                ins = V.tensor_scalar(xs4[:, 2 + c:3 + c],
                                      st[:, CB[c] + 14:CB[c] + 15], INV_QMAX,
                                      None, OP.mult)
            sV.inc(ins)
            state["xn_ready"] = sV.n

        return [L0, mk_xnS(0), mk_xnS(1), L3, L4, L5]

    def emit_ln(t):
        ea0, ea1, eb0, eb1, e1, e2 = emit_ln_early(t)
        l0, x0, x1, l3, l4, l5 = emit_ln_late(t)
        ea0(); ea1(); eb0(); eb1(); e1(); e2(); l0()
        S.activation(st[:, 7:8], st[:, 7:8], AF.Copy)
        x0(); x1(); l3(); l4(); l5()

    sX.inc(SY.dma_start(
        x_buf[0][:].rearrange("p (c d) -> p c d", d=D),
        x_d[0:T, :].rearrange("(c p) d -> p c d", p=128)), 16)

    for ap_s, ap_d in ((ident, ident_d), (identb, identb_d), (maskx4, mask_d),
                       (ones1, ones_d)):
        sC.inc(SY.dma_start(ap_s[:], ap_d[:]), 16)
    consts_ready = sC.n

    for n in wT_names:
        for kc in range(NF):
            sW.inc(G.dma_start(wr[n][:, kc * D:(kc + 1) * D],
                               wT_d[n][kc * 128:(kc + 1) * 128, :]), 16)
    w_ready = {n: 16 * NF * (i + 1) for i, n in enumerate(wT_names)}

    xn_fm32 = sb("xn_fm32", [128, NF * T])
    xn_fmb = sb("xn_fmb", [128, NF * T], BF16)
    xq_fmb = sb("xq_fmb", [128, NF * T], BF16)
    qk_b = sb("qk_b", [128, NF * T], BF16)
    vin_b = sb("vin_b", [128, NF * T], BF16)
    q_b = sb("q_b", [128, NF * T], BF16)
    k_b = sb("k_b", [128, NF * T], BF16)
    vtok_b = sb("vtok_b", [128, U * D], BF16)
    g_s = sb("g_s", [128, NF * T])
    o_b = sb("o_b", [128, NF * T], BF16)
    tmp_fm = [sb(f"tmp_fm{i}", [128, T]) for i in range(2)]
    sums = sb("sums", [128, 8])
    recs = sb("recs", [128, 8])
    A_b = sb("A_b", [128, 1024], BF16)
    AT_s = sb("AT_s", [128, NF * 128], BF16)
    bv2b = sb("bv2b", [128, NF], BF16)
    bfold = sb("bfold", [128, NF])
    sg_s = [sb(f"sg{i}", [128, T], BF16) for i in range(3)]

    stg = [g_s[:, 0:D], g_s[:, D:2 * D]]
    wqn = [o_b[:, 0:D], o_b[:, D:2 * D]]
    sC.inc(SY.dma_start(stg[0], wffn_d[0:128, :]), 16)
    sC.inc(SY.dma_start(stg[1], wffn_d[128:256, :]), 16)
    stg_ready = sC.n
    if n_tiles > 1:
        sX.inc(SY.dma_start(
            x_buf[1][:].rearrange("p (c d) -> p c d", d=D),
            x_d[T:2 * T, :].rearrange("(c p) d -> p c d", p=128)), 16)
    for n in b_names:
        sC.inc(SY.dma_start(bias[n][:], b_d[n][:]), 16)

    load_natural_exp()
    emit_ln(0)

    pstate = {"q": None, "tr": None, "stg": stg_ready}

    def prep_pair(op):
        if op > 0:
            if pstate["q"] is not None:
                SY.wait_ge(sV.h, pstate["q"])
            for j in (0, 1):
                sC.inc(SY.dma_start(stg[j],
                                    wffn_d[(op + j) * 128:(op + j + 1) * 128, :]),
                       16)
            pstate["stg"] = sC.n
        V.wait_ge(sC.h, pstate["stg"])
        for j in (0, 1):
            V.reduce_max(st[:, CB[j] + 13:CB[j] + 14], stg[j], axis=AX.X,
                         apply_absolute_value=True)
        for j in (0, 1):
            V.tensor_scalar(st[:, CB[j] + 14:CB[j] + 15],
                            st[:, CB[j] + 13:CB[j] + 14], 1e-8, None, OP.max)
        for j in (0, 1):
            V.reciprocal(st[:, CB[j] + 10:CB[j] + 11],
                         st[:, CB[j] + 14:CB[j] + 15])
        for j in (0, 1):
            V.tensor_scalar(st[:, CB[j] + 9:CB[j] + 10],
                            st[:, CB[j] + 10:CB[j] + 11], 127.0, None, OP.mult)
        for j in (0, 1):
            V.tensor_scalar(wsc[:, op + j:op + j + 1],
                            st[:, CB[j] + 14:CB[j] + 15], INV_QMAX, None, OP.mult)
        if pstate["tr"] is not None:
            V.wait_ge(sT.h, pstate["tr"])
        for j in (0, 1):
            V.tensor_scalar(stg[j], stg[j], st[:, CB[j] + 9:CB[j] + 10],
                            MAGIC, OP.mult, OP.add)
        for j in (0, 1):
            ins = V.tensor_scalar(wqn[j], stg[j], -MAGIC, None, OP.add)
        sV.inc(ins)
        pstate["q"] = sV.n

        sV.wait(TE)
        if op == 0:
            TE.wait_ge(sC.h, 16 * 4)
        for j in (0, 1):
            oc = op + j
            bk = 6 + j
            te_guard(bk)
            for kc in range(NF):
                ins = TE.transpose(PB[bk][:, kc * 128:(kc + 1) * 128],
                                   wqn[j][kc * 128:(kc + 1) * 128]
                                   if False else
                                   o_b[:, j * D + kc * 128: j * D + (kc + 1) * 128],
                                   identb[:])
            sT.inc(ins)
            sT.wait(S)
            dst = wr["wffnT"][:].rearrange("p (k d) -> p k d", d=D)[
                :, :, oc * 128:(oc + 1) * 128]
            sS.inc(S.activation(dst, PB[bk][:].rearrange("p (k d) -> p k d",
                                                         d=128), AF.Copy))
            guard[bk] = (sS, sS.n)
        pstate["tr"] = sT.n
        state["ffn_ready_T"] = sS.n
        state["ffn_ready_V"] = sV.n

    prep_pair(0)
    prep_parts = [lambda op=op: prep_pair(op) for op in (2, 4, 6)]

    for t in range(n_tiles):
        xb = x_buf[t % 2]
        xn_ready = state["xn_ready"]

        TE.wait_ge(sV.h, xn_ready)
        te_guard(4)
        sT.inc(TE.transpose(PB[4][0:4, 0:128], xs4[:], ident[:]))
        state["ln_guard"] = sT.n
        sT.wait(S)
        sS.inc(S.activation(xsT[:], PB[4][0:4, 0:128], AF.Copy))
        guard[4] = (sS, sS.n)
        sS.wait(SY)
        sR.inc(SY.dma_start(xsrow[:], xsT[:]), 16)

        xn_ev = []
        XNB = [2, 3, 0, 1]
        for fp in range(4):
            b = XNB[fp]
            te_guard(b)
            for j in range(2):
                f = fp * 2 + j
                for c in range(U):
                    ins = TE.transpose(
                        PB[b][:, j * 256 + c * 128: j * 256 + c * 128 + 128],
                        xn_tok[c][:, f * 128:(f + 1) * 128], ident[:])
            sT.inc(ins)
            sT.wait(S)
            f0 = fp * 2
            ins1 = S.activation(xn_fmb[:, f0 * T:(f0 + 2) * T], PB[b][:],
                                AF.Copy)
            sS.inc(ins1)
            xn_ev.append(sS.n)
            ins2 = S.activation(xn_fm32[:, f0 * T:(f0 + 2) * T], PB[b][:],
                                AF.Copy)
            sS.inc(ins2)
            guard[b] = (sS, sS.n)
        xn_fm_ready = xn_ev[-1]
        xn32_ready = sS.n

        def proj(dst, wname, bname, rhs_buf, kc_waits=None, rot=0, rhs_ev=None):
            if kc_waits is None:
                TE.wait_ge(sS.h, rhs_ev)
            for m in range(NF):
                b = 2 + ((m + rot) % 4)
                te_guard(b)
                for kc in range(NF):
                    if kc_waits is not None and m == 0:
                        TE.wait_ge(sS.h, kc_waits[kc // 2])
                    ins = TE.matmul(
                        PB[b][:, 0:T],
                        wr[wname][:, kc * D + m * 128: kc * D + m * 128 + 128],
                        rhs_buf[:, kc * T:(kc + 1) * T],
                        start=(kc == 0), stop=(kc == NF - 1))
                sT.inc(ins)
                sT.wait(S)
                sS.inc(S.activation(dst[:, m * T:(m + 1) * T], PB[b][:, 0:T],
                                    AF.Identity, bias=bias[bname][:, m:m + 1]))
                guard[b] = (sS, sS.n)

        if t == 0:
            sC.wait(S)
        TE.wait_ge(sW.h, w_ready["wqkT"])
        proj(qk_b, "wqkT", "bqk", xn_fmb, kc_waits=xn_ev)
        qk_ev = sS.n
        load_natural_exp()

        te_guard(2)
        sR.wait(TE)
        for c in range(U):
            TE.matmul(PB[2][:, c * 128:(c + 1) * 128], ones1[:],
                      xsrow[0:1, c * 128:(c + 1) * 128], start=True, stop=True)
            ins = TE.matmul(PB[2][:, 256 + c * 128:256 + (c + 1) * 128], ones1[:],
                            xsrow[0:1, 256 + c * 128:256 + (c + 1) * 128],
                            start=True, stop=True)
        sT.inc(ins)
        sT.wait(V)
        sV.inc(V.tensor_copy(invxs[:], PB[2][:]))
        guard[2] = (sV, sV.n)
        invxs_ready = sV.n

        V.wait_ge(sV.h, invxs_ready)
        V.wait_ge(sS.h, xn32_ready)
        V.tensor_scalar(st[:, 11:12], st[:, 11:12], 1.0, None, OP.mult)
        for m in range(NF):
            V.tensor_mul(tmp_fm[m % 2][:], xn_fm32[:, m * T:(m + 1) * T],
                         invxs[:, 0:T])
            if m > 0:
                V.tensor_scalar(xq_fmb[:, (m - 1) * T:m * T],
                                tmp_fm[(m - 1) % 2][:], MAGIC, -MAGIC,
                                OP.add, OP.add)
        sV.inc(V.tensor_scalar(xq_fmb[:, (NF - 1) * T:NF * T],
                               tmp_fm[(NF - 1) % 2][:], MAGIC, -MAGIC,
                               OP.add, OP.add))
        xq_ready = sV.n

        ln_early = emit_ln_early(t + 1) if t + 1 < n_tiles else []
        if ln_early:
            ln_early[0]()
        TE.wait_ge(sW.h, w_ready["wvT"])
        proj(vin_b, "wvT", "bv", xn_fmb, rot=1, rhs_ev=xn_fm_ready)
        vin_ev = sS.n
        if t == 0:
            prep_parts.pop(0)()
        if ln_early:
            ln_early[1]()
        TE.wait_ge(sW.h, w_ready["wqT"])
        proj(q_b, "wqT", "bq", qk_b, rhs_ev=qk_ev)
        if t == 0:
            prep_parts.pop(0)()
        if ln_early:
            ln_early[2]()
        TE.wait_ge(sW.h, w_ready["wkT"])
        proj(k_b, "wkT", "bk", qk_b, rhs_ev=qk_ev)
        k_ev = sS.n
        if t == 0:
            prep_parts.pop(0)()
        if ln_early:
            ln_early[3]()
            ln_early[4]()

        TE.wait_ge(sW.h, w_ready["wv2T"])
        TE.wait_ge(sS.h, vin_ev)
        for c in range(U):
            for half in range(2):
                b = 2 + ((2 * c + half) % 4)
                te_guard(b)
                for kc in range(NF):
                    ins = TE.matmul(
                        PB[b][:, 0:512],
                        vin_b[:, kc * T + c * 128: kc * T + c * 128 + 128],
                        wr["wv2T"][:, kc * D + half * 512: kc * D + half * 512 + 512],
                        start=(kc == 0), stop=(kc == NF - 1))
                sT.inc(ins)
                sT.wait(S)
                sS.inc(S.activation(
                    vtok_b[:, c * D + half * 512: c * D + half * 512 + 512],
                    PB[b][:, 0:512], AF.Copy))
                guard[b] = (sS, sS.n)

        TE.wait_ge(sS.h, k_ev)
        expb = {0: xn_fm32[:, 0:1024], 1: xn_fm32[:, 1024:2048]}

        def scores(c):
            for hh in range(2):
                te_guard(4 + hh)
                TE.matmul(PB[4 + hh][:, 0:512], identb[:], maskx4[:],
                          start=True, stop=False)
                for j in range(4):
                    h = hh * 4 + j
                    ins = TE.matmul(PB[4 + hh][:, j * 128:(j + 1) * 128],
                                    q_b[:, h * T + c * 128: h * T + c * 128 + 128],
                                    k_b[:, h * T + c * 128: h * T + c * 128 + 128],
                                    start=False, stop=(j == 3))
                sT.inc(ins)
            return sT.n

        def expc(c, sc_cnt):
            S.wait_ge(sT.h, sc_cnt)
            exs = []
            for hh in range(2):
                ins = S.activation(expb[c][:, hh * 512:(hh + 1) * 512],
                                   PB[4 + hh][:], AF.Exp, scale=SCALE)
                sS.inc(ins)
                exs.append(sS.n)
                guard[4 + hh] = (sS, sS.n)
            return exs

        def soft2(c, exp_cnts):
            V.wait_ge(sS.h, exp_cnts[0])
            V.reduce_sum(sums[:, 0:4],
                         expb[c][:, 0:512].rearrange("p (a b) -> p a b", a=4),
                         axis=AX.X)
            V.wait_ge(sS.h, exp_cnts[1])
            V.reduce_sum(sums[:, 4:8],
                         expb[c][:, 512:1024].rearrange("p (a b) -> p a b",
                                                        a=4), axis=AX.X)
            V.reciprocal(recs[:, 0:4], sums[:, 0:4])
            V.reciprocal(recs[:, 4:8], sums[:, 4:8])
            if state["at_tr"] is not None:
                V.wait_ge(sT.h, state["at_tr"])
            nms = []
            for hh in range(2):
                for j in range(4):
                    ins = V.tensor_scalar(A_b[:, hh * 512 + j * 128:
                                              hh * 512 + (j + 1) * 128],
                                          expb[c][:, hh * 512 + j * 128:
                                               hh * 512 + (j + 1) * 128],
                                          recs[:, 4 * hh + j:4 * hh + j + 1],
                                          None, OP.mult)
                sV.inc(ins)
                nms.append(sV.n)
            return nms

        def at_tr_f(c, norm_cnts):
            trs = []
            for hh in range(2):
                TE.wait_ge(sV.h, norm_cnts[hh])
                te_guard(6 + hh)
                for j in range(4):
                    ins = TE.transpose(PB[6 + hh][:, j * 128:(j + 1) * 128],
                                       A_b[:, hh * 512 + j * 128:
                                           hh * 512 + (j + 1) * 128], identb[:])
                sT.inc(ins)
                trs.append(sT.n)
            state["at_tr"] = sT.n
            return trs

        def at_ev(c, tr_cnts):
            evs = []
            for hh in range(2):
                S.wait_ge(sT.h, tr_cnts[hh])
                ins = S.activation(AT_s[:, hh * 512:(hh + 1) * 512],
                                   PB[6 + hh][:, 0:512], AF.Copy)
                sS.inc(ins)
                evs.append(sS.n)
                guard[6 + hh] = (sS, sS.n)
            return evs

        def o_mm(c, ev_cnts):
            for hh in range(2):
                TE.wait_ge(sS.h, ev_cnts[hh])
                te_guard(2 + hh)
                for j in range(4):
                    h = hh * 4 + j
                    ins = TE.matmul(PB[2 + hh][:, j * 128:(j + 1) * 128],
                                    vtok_b[:, c * D + h * 128: c * D + h * 128 + 128],
                                    AT_s[:, h * 128:(h + 1) * 128],
                                    start=True, stop=True)
                sT.inc(ins)
            return sT.n

        def o_ev(c, mm_cnt):
            S.wait_ge(sT.h, mm_cnt)
            for hh in range(2):
                src_ = PB[2 + hh][:].rearrange("p (j x) -> p j x", x=128)
                dst_ = o_b[:].rearrange("p (h x) -> p h x", x=T)[
                    :, 4 * hh:4 * hh + 4, c * 128:(c + 1) * 128]
                ins = S.activation(dst_, src_, AF.Copy)
            sS.inc(ins)
            guard[2] = (sS, sS.n)
            guard[3] = (sS, sS.n)
            return sS.n

        ffn_mm_cnt = {}
        ffn_sil = {}
        ffn_slot_free = {}
        ffn_last_mm = {0: None, 1: None}

        def ffn_slot(m):
            return m % 2, (T if (m % 4) >= 2 else 0)

        def ffn_mm(m):
            if t == 0 and m == 0:
                TE.wait_ge(sS.h, state["ffn_ready_T"])
            if m == 0:
                TE.wait_ge(sV.h, xq_ready)
            b, co = ffn_slot(m)
            if m < 2:
                te_guard(b)
            elif m >= 4:
                TE.wait_ge(sV.h, ffn_slot_free[m % 4])
            for kc in range(NF):
                ins = TE.matmul(
                    PB[b][:, co:co + T],
                    wr["wffnT"][:, kc * D + m * 128: kc * D + m * 128 + 128],
                    xq_fmb[:, kc * T:(kc + 1) * T],
                    start=(kc == 0), stop=(kc == NF - 1))
            sT.inc(ins)
            ffn_mm_cnt[m] = sT.n
            ffn_last_mm[b] = sT.n

        def ffn_ev(m):
            if t == 0 and m == 0:
                V.wait_ge(sV.h, state["ffn_ready_V"])
            b, co = ffn_slot(m)
            V.wait_ge(sT.h, ffn_last_mm[b])
            sV.inc(V.scalar_tensor_tensor(g_s[:, m * T:(m + 1) * T],
                                          PB[b][:, co:co + T],
                                          wsc[:, m:m + 1], invxs[:, 256:256 + T],
                                          OP.mult, OP.mult))
            ffn_slot_free[m % 4] = sV.n
            guard[b] = (sV, sV.n)

        def ffn_si(m):
            sV.wait(S)
            sS.inc(S.activation(g_s[:, m * T:(m + 1) * T],
                                g_s[:, m * T:(m + 1) * T],
                                AF.Silu, bias=bias["bffn"][:, m:m + 1]))
            ffn_sil[m] = sS.n

        if t == 0:
            for eng in (TE, V, S):
                sC.wait(eng)
                sW.wait(eng)
                sX.wait(eng)
        if ln_early:
            ln_early[5]()
        sc0 = scores(0)
        ffn_mm(0)
        ffn_mm(1)
        ex0 = expc(0, sc0)
        nm0 = soft2(0, ex0)
        sc1 = scores(1)
        ffn_mm(2)
        ffn_mm(3)
        tr0 = at_tr_f(0, nm0)
        ex1 = expc(1, sc1)
        ffn_ev(0)
        ffn_ev(1)
        ev0 = at_ev(0, tr0)
        S.preload_act_table(AF.Silu)
        ffn_mm(4)
        ffn_mm(5)
        om0 = o_mm(0, ev0)
        nm1 = soft2(1, ex1)
        oe0 = o_ev(0, om0)
        ffn_si(0)
        ffn_si(1)
        tr1 = at_tr_f(1, nm1)
        ffn_ev(2)
        ffn_ev(3)
        ev1 = at_ev(1, tr1)
        ffn_mm(6)
        om1 = o_mm(1, ev1)
        ffn_mm(7)
        ffn_ev(4)
        ffn_ev(5)
        oe1 = o_ev(1, om1)
        o_ready = sS.n
        ffn_si(2)
        ffn_si(3)
        ffn_ev(6)
        ffn_ev(7)
        ffn_si(4)
        ffn_si(5)
        ffn_si(6)
        ffn_si(7)

        if t == 0:
            TE.wait_ge(sW.h, w_ready["woT"])
            sC.wait(V)
            sV.inc(V.tensor_copy(bv2b[:], bias["bv2"][:]))
            sV.wait(TE)
            te_guard(3)
            for m in range(NF):
                for kc in range(NF):
                    ins = TE.matmul(
                        PB[3][:, m:m + 1],
                        wr["woT"][:, kc * D + m * 128: kc * D + (m + 1) * 128],
                        bv2b[:, kc:kc + 1],
                        start=(kc == 0), stop=(kc == NF - 1))
            sT.inc(ins)
            sT.wait(V)
            sV.inc(V.tensor_add(bfold[:], PB[3][:, 0:8], bias["bo"][:]))
            guard[3] = (sV, sV.n)
            V.tensor_scalar(st[:, 11:12], st[:, 11:12], 1.0, None, OP.mult)
        ln_late = emit_ln_late(t + 1) if t + 1 < n_tiles else []
        TE.wait_ge(sW.h, w_ready["woT"])
        TE.wait_ge(sS.h, o_ready)
        sg_tr = {}
        sg_stt = {}

        def sg_transpose(m):
            TE.wait_ge(sV.h, sg_stt[m])
            if m == 0:
                te_guard(6)
                te_guard(7)
            for c in range(U):
                ins = TE.transpose(PB[6 + c][:, m * 128:(m + 1) * 128],
                                   sg_s[m % 3][:, c * 128:(c + 1) * 128],
                                   identb[:])
            sT.inc(ins)
            sg_tr[m] = sT.n

        for m in range(NF):
            b = 2 + (m % 2)
            te_guard(b)
            for kc in range(NF):
                ins = TE.matmul(
                    PB[b][:, 0:T],
                    wr["woT"][:, kc * D + m * 128: kc * D + m * 128 + 128],
                    o_b[:, kc * T:(kc + 1) * T],
                    start=(kc == 0), stop=(kc == NF - 1))
            sT.inc(ins)
            if m >= 3:
                sg_transpose(m - 3)
            sT.wait(V)
            V.wait_ge(sS.h, ffn_sil[m])
            if m >= 3:
                V.wait_ge(sT.h, sg_tr[m - 3])
            sV.inc(V.scalar_tensor_tensor(sg_s[m % 3][:], PB[b][:, 0:T],
                                          bfold[:, m:m + 1],
                                          g_s[:, m * T:(m + 1) * T],
                                          OP.add, OP.mult))
            guard[b] = (sV, sV.n)
            sg_stt[m] = sV.n
            if ln_late:
                ln_late.pop(0)()
        sg_transpose(NF - 3)
        sg_transpose(NF - 2)
        sg_transpose(NF - 1)
        while ln_late:
            ln_late.pop(0)()
        sT.wait(V)
        for c in range(U):
            ins = V.tensor_add(xb[:, c * D:(c + 1) * D],
                               xb[:, c * D:(c + 1) * D], PB[6 + c][:])
        sV.inc(ins)
        guard[6] = (sV, sV.n)
        guard[7] = (sV, sV.n)
        fin_done = sV.n

        SY.wait_ge(sV.h, fin_done)
        sO.inc(SY.dma_start(
            y_d[t * T:(t + 1) * T, :].rearrange("(c p) d -> p c d", p=128),
            xb[:].rearrange("p (c d) -> p c d", d=D)), 16)
        if t + 2 < n_tiles:
            SY.wait_ge(sO.h, 16 * (t + 1))
            sX.inc(SY.dma_start(
                x_buf[t % 2][:].rearrange("p (c d) -> p c d", d=D),
                x_d[(t + 2) * T:(t + 3) * T, :].rearrange(
                    "(c p) d -> p c d", p=128)), 16)

    if debug:
        dbg = {
            "d_xn0": xn_tok[0], "d_xn1": xn_tok[1], "d_xnfmb": xn_fmb,
            "d_xnfm32": xn_fm32, "d_xqfmb": xq_fmb, "d_qk": qk_b,
            "d_vin": vin_b, "d_q": q_b, "d_k": k_b, "d_vtok": vtok_b,
            "d_o": o_b, "d_g": g_s, "d_invxs": invxs, "d_Ab": A_b,
            "d_ATs": AT_s, "d_xs4": xs4, "d_sums": sums,
            "d_wffnT": wr["wffnT"], "d_wsc": wsc, "d_st": st,
        }
        SY.wait_ge(sT.h, sT.n)
        SY.wait_ge(sV.h, sV.n)
        SY.wait_ge(sS.h, sS.n)
        sD = Ctr(es.enter_context(nc.semaphore("semD")))
        for nm, tt in dbg.items():
            shape = list(tt.ap().shape)
            dd = nc.dram_tensor(nm, shape, tt.dtype, kind="ExternalOutput").ap()
            sD.inc(SY.dma_start(dd[:], tt[:]), 16)
        SY.wait_ge(sD.h, sD.n)
    SY.wait_ge(sO.h, 16 * n_tiles)
    es.close()
    return nc


def _host_inputs(inputs):
    f32 = np.float32
    x = np.ascontiguousarray(np.asarray(inputs["L0_attn_x"], f32).reshape(-1, D))
    qk_w = np.asarray(inputs["L0_attn_proj_qk_w"], f32)
    v_w = np.asarray(inputs["L0_attn_proj_v_w"], f32)
    in_w = np.asarray(inputs["L0_attn_in_proj_w"], f32)
    out_w = np.asarray(inputs["L0_attn_out_proj_w"], f32)
    ffn_w = np.ascontiguousarray(np.asarray(inputs["L0_ffn_weight"], f32))
    tr = lambda w: np.ascontiguousarray(w.T)
    bre = lambda b: np.ascontiguousarray(np.asarray(b, f32).reshape(NF, 128).T)
    in_b = np.asarray(inputs["L0_attn_in_proj_b"], f32)

    mask = np.full((128, 128), -1.0e30, f32)
    for q in range(128):
        s0 = (q // 4) * 4
        for k in range(s0, s0 + 4):
            if k != q:
                mask[q, k] = 0.0
    common = {
        "wqkT": tr(qk_w), "wvT": tr(v_w),
        "wqT": tr(in_w[0:D]), "wkT": tr(in_w[D:2 * D]),
        "wv2T": tr(in_w[2 * D:3 * D]),
        "woT": tr(out_w), "wffn": ffn_w,
        "bqk": bre(inputs["L0_attn_proj_qk_b"]),
        "bv": bre(inputs["L0_attn_proj_v_b"]),
        "bq": bre(in_b[0:D]), "bk": bre(in_b[D:2 * D]), "bv2": bre(in_b[2 * D:3 * D]),
        "bo": bre(inputs["L0_attn_out_proj_b"]), "bffn": bre(inputs["L0_ffn_bias"]),
        "ident": np.eye(128, dtype=f32),
        "identb": np.eye(128, dtype=f32).astype(ml_dtypes.bfloat16),
        "maskx4": np.ascontiguousarray(np.tile(mask, (1, 4))).astype(
            ml_dtypes.bfloat16),
        "ones1": np.ones((1, 128), f32),
    }
    return x, common


def kernel(**inputs):
    x, common = _host_inputs(inputs)
    nc = build_nc(NTILES)
    in_maps = []
    for i in range(NCORES):
        m = dict(common)
        m["x"] = np.ascontiguousarray(x[i * ROWS:(i + 1) * ROWS])
        in_maps.append(m)
    res = run_bass_kernel_spmd(nc, in_maps, core_ids=list(range(NCORES)))
    y = np.concatenate([np.asarray(res.results[i]["y"]) for i in range(NCORES)],
                       axis=0)
    return y.reshape(2, 8192, D)
